# version 05 (final)
# baseline (speedup 1.0000x reference)
import os
from contextlib import ExitStack

import numpy as np
import ml_dtypes

import concourse.bass as bass
import concourse.mybir as mybir
from concourse.bass_utils import run_bass_kernel_spmd

F32 = mybir.dt.float32
BF16 = mybir.dt.bfloat16
U8 = mybir.dt.uint8
I32 = mybir.dt.int32
AF = mybir.ActivationFunctionType
ALU = mybir.AluOpType
AX = mybir.AxisListType.X

HID = 4096
INTER = 11008
TOK = 2048
T = 512
NTT = TOK // T
D = 1024
QKVD = 3072
S = 8192
NKC = HID // 128

DEBUG_STOP = os.environ.get("MK_DEBUG_STOP", "full")
DEBUG_CACHE = os.environ.get("MK_DEBUG_CACHE", "0") == "1"
DEBUG_S0 = os.environ.get("MK_DEBUG_S0", "0") == "1"
DEBUG_FFN = os.environ.get("MK_DEBUG_FFN", "0") == "1"
DBG_ITEM = int(os.environ.get("MK_DBG_ITEM", "0"))


def sigma8_perm(K):
    perm = []
    nfull, rem = divmod(K, 1024)
    for c in range(nfull):
        for j in range(8):
            perm.extend(1024 * c + 8 * p + j for p in range(128))
    if rem:
        P = rem // 8
        for j in range(8):
            perm.extend(1024 * nfull + 8 * p + j for p in range(P))
    return np.asarray(perm)


def chunks_of(K):
    out, pos = [], 0
    nfull, rem = divmod(K, 1024)
    for c in range(nfull):
        for j in range(8):
            out.append((c, j, 128, pos)); pos += 128
    if rem:
        for j in range(8):
            out.append((nfull, j, rem // 8, pos)); pos += rem // 8
    assert pos == K
    return out


PERM_D = sigma8_perm(HID)
PERM_I = sigma8_perm(INTER)
INV_D = np.argsort(PERM_D)
CH_D = chunks_of(HID)
CH_I = chunks_of(INTER)
NIC = len(CH_I)
ICW = 128
CGROW = NKC * 128
CDROW = NIC * ICW


def byte_plane(qw):
    K8, N = qw.shape
    return np.ascontiguousarray(
        qw.view(np.uint8).reshape(K8, N, 4).transpose(0, 2, 1).reshape(K8 * 4, N))


ENGINES = ("sync", "scalar", "vector", "tensor", "gpsimd")


class Sem:
    __slots__ = ("h", "name")

    def __init__(self, h, name):
        self.h, self.name = h, name


class M:

    def __init__(self, counts, real=None, eng=None):
        self.c = counts
        self.real = real
        self._eng = eng
        self.rings = {}

    def on(self, name):
        return self.real == name

    def e(self):
        return self._eng

    def wait(self, eng, sem, val=None):
        if val is None:
            val = self.c[sem]
        if val <= 0:
            return
        if self.on(eng):
            self._eng.wait_ge(sem.h, val)

    def inc(self, sem, instr, k):
        if instr is not None:
            instr.then_inc(sem.h, k)
        self.c[sem] += k

    def dma(self, eng, dst, src, sem, **kw):
        instr = self._eng.dma_start(dst, src, **kw) if self.on("gpsimd") else None
        self.inc(sem, instr, 16)

    def op(self, eng, fn, sem=None, k=1):
        instr = fn(self._eng) if self.on(eng) else None
        if sem is not None:
            self.inc(sem, instr, k)

    def ring(self, name, depth, sem):
        r = self.rings.get(name)
        if r is None:
            r = self.rings[name] = _Ring(depth, sem)
        return r

    def acq(self, eng, name, depth, sem):
        r = self.ring(name, depth, sem)
        k = r.p
        r.p += 1
        if k >= depth:
            self.wait(eng, sem, r.marks[k - depth])
        return k % depth

    def rel(self, name):
        r = self.rings[name]
        r.marks.append(self.c[r.sem])

    def pad(self, eng, n=1):
        for _ in range(n):
            if eng == "scalar":
                self.op(eng, lambda e: e.activation(
                    self.padS[1][:, :], self.padS[0][:, :], AF.Copy))
            else:
                self.op(eng, lambda e: e.tensor_copy(
                    self.padV[1][:, :], self.padV[0][:, :]))

    def barrier(self, bar):
        if self.real is not None:
            self._eng.sem_inc(bar.h, 1)
        self.c[bar] += 5
        tgt = self.c[bar]
        if self.real is not None:
            self._eng.wait_ge(bar.h, tgt)


class _Ring:
    __slots__ = ("depth", "sem", "marks", "p")

    def __init__(self, depth, sem):
        self.depth, self.sem, self.marks, self.p = depth, sem, [], 0


PAD_TILES = {}


def run_block(nc, counts, schedule):
    snapshot = dict(counts)
    result = {}
    with nc.Block() as block:
        for engname in ENGINES:
            def make(engname=engname):
                def _body(e):
                    m = M(dict(snapshot), real=engname, eng=e)
                    m.padV = PAD_TILES.get("V")
                    m.padS = PAD_TILES.get("S")
                    schedule(m)
                    result.update(m.c)
                return _body
            getattr(block, engname)(make())
    counts.clear()
    counts.update(result)


class _Ctx(dict):
    def __getattr__(self, k):
        try:
            return self[k]
        except KeyError as e:
            raise AttributeError(k) from e

    def __setattr__(self, k, v):
        self[k] = v


_UNIQ = [0]


def _sb(nc, st, shape, dt):
    _UNIQ[0] += 1
    return st.enter_context(nc.sbuf_tensor(f"sb{_UNIQ[0]}", shape, dt))


def _ps(nc, st, shape, dt):
    _UNIQ[0] += 1
    return st.enter_context(nc.psum_tensor(f"ps{_UNIQ[0]}", shape, dt))


def stage0_schedule(t, m):
    item = 0
    for mat in ("gate", "up", "down"):
        if mat != "down":
            src_bp, src_sc, src_qz = t[mat + "_bp"], t[mat + "_sc"], t[mat + "_qz"]
            cache = t.CG if mat == "gate" else t.CU
            NDIM, n_tiles, qzc, crow = INTER, [2048] * 5 + [768], 1376, CGROW
            n_ci, halfscale = 4, False
        else:
            src_bp, src_sc, src_qz = t.down_bp, t.down_sc, t.down_qz
            cache, NDIM, n_tiles, qzc, crow = t.CD, HID, [2048, 2048], 512, CDROW
            n_ci, halfscale = 11, True
        for ci in range(n_ci):
            P = 96 if (mat == "down" and ci == 10) else 128
            ng = (8 * P) // 128
            ntoff = 0
            for nt, NW in enumerate(n_tiles):
                i = item
                sl = m.acq("gpsimd", "s0set", 2, t.s_dq)
                for b in range(4):
                    m.dma("sync", t.plane[sl][b][0:P, 0:NW],
                          bass.AP(src_bp, (512 * ci + b) * NDIM + ntoff,
                                  [[4 * NDIM, P], [1, NW]]), t.s_ld)
                m.dma("sync", t.srows[sl][0:ng, 0:NW],
                      bass.AP(src_sc, 8 * ci * NDIM + ntoff, [[NDIM, ng], [1, NW]]),
                      t.s_ld)
                m.dma("sync", t.qzt[sl][0:ng, 0:qzc],
                      bass.AP(src_qz, 8 * ci * qzc, [[qzc, ng], [1, qzc]]), t.s_ld)
                lds = m.c[t.s_ld]
                m.wait("vector", t.s_ld, lds)
                m.wait("vector", t.s_bc, max(0, i - 1))
                pieces = []
                for j in range(8):
                    if NW == 2048:
                        pieces += [(c, j, 128 * (8 * c + j) - 2048 * nt, 128)
                                   for c in (2 * nt, 2 * nt + 1)]
                    else:
                        pieces.append((10, j, 96 * j, 96))
                zoff = 0
                zoffs = []
                for (c, j, doff, w) in pieces:
                    m.op("vector", lambda e, sl=sl, ng=ng, c=c, j=j, w=w, zo=zoff:
                         e.tensor_scalar(t.zall[0:ng, zo:zo + w],
                                         t.qzt[sl][0:ng, 128 * c:128 * c + w],
                                         4 * j, 15,
                                         ALU.logical_shift_right, ALU.bitwise_and))
                    zoffs.append(zoff)
                    zoff += w
                for (c, j, doff, w), zo in zip(pieces, zoffs):
                    m.op("vector", lambda e, sl=sl, ng=ng, doff=doff, w=w, zo=zo:
                         e.tensor_scalar(t.crows[sl][0:ng, doff:doff + w],
                                         t.zall[0:ng, zo:zo + w],
                                         -1.0, -1.0, ALU.mult, ALU.add))
                if halfscale:
                    m.op("vector", lambda e, sl=sl, ng=ng, NW=NW:
                         e.tensor_scalar(t.srows2[sl][0:ng, 0:NW],
                                         t.srows[sl][0:ng, 0:NW],
                                         0.5, None, ALU.mult))
                else:
                    m.op("vector", lambda e, sl=sl, ng=ng, NW=NW:
                         e.tensor_copy(t.srows2[sl][0:ng, 0:NW],
                                       t.srows[sl][0:ng, 0:NW]))
                m.op("vector", lambda e, sl=sl, ng=ng, NW=NW:
                     e.tensor_copy(t.shi[sl][0:ng, 0:NW], t.srows2[sl][0:ng, 0:NW]))
                m.op("vector", lambda e, sl=sl, ng=ng, NW=NW:
                     e.tensor_tensor(t.slo[sl][0:ng, 0:NW], t.srows2[sl][0:ng, 0:NW],
                                     t.shi[sl][0:ng, 0:NW], ALU.subtract))
                m.pad("vector", 2)
                m.op("vector", lambda e: e.tensor_copy(
                    t.mark[0:1, 0:1], t.mark[0:1, 1:2]), t.s_cr, 1)
                m.wait("tensor", t.s_cr, m.c[t.s_cr])
                m.wait("tensor", t.s_dq, i)
                nmm = (NW + 511) // 512
                for q in range(nmm):
                    w = min(512, NW - 512 * q)
                    m.op("tensor", lambda e, sl=sl, q=q, w=w, ng=ng, P=P:
                         e.matmul(t.p_sb[0:P, 512 * q:512 * q + w],
                                  t.ind8b[0:ng, 0:P],
                                  t.shi[sl][0:ng, 512 * q:512 * q + w],
                                  start=True, stop=False))
                    m.op("tensor", lambda e, sl=sl, q=q, w=w, ng=ng, P=P:
                         e.matmul(t.p_sb[0:P, 512 * q:512 * q + w],
                                  t.ind8b[0:ng, 0:P],
                                  t.slo[sl][0:ng, 512 * q:512 * q + w],
                                  start=False, stop=True))
                for q in range(nmm):
                    w = min(512, NW - 512 * q)
                    m.op("tensor", lambda e, sl=sl, q=q, w=w, ng=ng, P=P:
                         e.matmul(t.p_cb[0:P, 512 * q:512 * q + w],
                                  t.ind8b[0:ng, 0:P],
                                  t.crows[sl][0:ng, 512 * q:512 * q + w],
                                  start=True, stop=True),
                         t.s_bc if q == nmm - 1 else None, 1)
                m.wait("vector", t.s_bc, m.c[t.s_bc])
                for j in range(8):
                    b, h = j >> 1, j & 1
                    wsl = m.acq("vector", "s0w", 4, t.s_wr)
                    if h:
                        m.op("vector", lambda e, sl=sl, b=b, P=P, NW=NW:
                             e.tensor_scalar(t.vu8[0:P, 0:NW],
                                             t.plane[sl][b][0:P, 0:NW],
                                             4, None, ALU.logical_shift_right))
                    else:
                        m.op("vector", lambda e, sl=sl, b=b, P=P, NW=NW:
                             e.tensor_scalar(t.vu8[0:P, 0:NW],
                                             t.plane[sl][b][0:P, 0:NW],
                                             15, None, ALU.bitwise_and))
                    m.op("vector", lambda e, P=P, NW=NW:
                         e.tensor_tensor(t.vadd[0:P, 0:NW], t.vu8[0:P, 0:NW],
                                         t.p_cb[0:P, 0:NW], ALU.add))
                    m.op("vector", lambda e, P=P, NW=NW, wsl=wsl:
                         e.tensor_tensor(t.wt[wsl][0:P, 0:NW], t.vadd[0:P, 0:NW],
                                         t.p_sb[0:P, 0:NW], ALU.mult), t.s_wt, 1)
                    m.wait("gpsimd", t.s_wt, m.c[t.s_wt])
                    kcg = 8 * ci + j
                    if mat != "down":
                        if NW == 2048:
                            dst = bass.AP(cache, (16 * nt) * 128 * crow + kcg * 128,
                                          [[crow, 128], [128 * crow, 16], [1, 128]])
                            src = bass.AP(t.wt[wsl], 0, [[2048, 128], [128, 16], [1, 128]])
                        else:
                            dst = bass.AP(cache, 80 * 128 * crow + kcg * 128,
                                          [[crow, 128], [128 * crow, 8], [1, 96]])
                            src = bass.AP(t.wt[wsl], 0, [[2048, 128], [96, 8], [1, 96]])
                    else:
                        dst = bass.AP(cache, (16 * nt) * 128 * crow + kcg * 128,
                                      [[crow, P], [128 * crow, 16], [1, 128]])
                        src = bass.AP(t.wt[wsl], 0, [[2048, P], [128, 16], [1, 128]])
                    m.dma("gpsimd", dst, src, t.s_wr)
                    if os.environ.get("MK_SERIAL_CACHE", "0") == "1":
                        m.wait("gpsimd", t.s_wr, m.c[t.s_wr])
                    m.rel("s0w")
                if DEBUG_S0 and i == DBG_ITEM:
                    m.op("vector", lambda e: e.tensor_copy(t.dbg_cb[:, :], t.p_cb[:, :]))
                    m.op("vector", lambda e: e.tensor_copy(t.dbg_sb2[:, :], t.p_sb[:, :]))
                    m.op("vector", lambda e, sl=sl: e.tensor_copy(t.dbg_cr[:, :], t.crows[sl][:, :]))
                    m.op("vector", lambda e: e.tensor_copy(t.dbg_za[:, :], t.zall[:, :]))
                    m.op("vector", lambda e, sl=sl: e.tensor_copy(t.dbg_qz[:, :], t.qzt[sl][:, :]),
                         t.s_dbg, 1)
                    m.wait("gpsimd", t.s_dbg, m.c[t.s_dbg])
                    m.dma("gpsimd", t.dbgcb_d[:, :], t.dbg_cb[:, :], t.s_wr)
                    m.dma("gpsimd", t.dbgsb_d[:, :], t.dbg_sb2[:, :], t.s_wr)
                    m.dma("gpsimd", t.dbgcr_d[:, :], t.dbg_cr[:, :], t.s_wr)
                    m.dma("gpsimd", t.dbgza_d[:, :], t.dbg_za[:, :], t.s_wr)
                    m.dma("gpsimd", t.dbgqz_d[:, :], t.dbg_qz[:, :], t.s_wr)
                m.op("vector", lambda e: e.tensor_copy(t.mark[0:1, 0:1],
                                                       t.mark[0:1, 1:2]), t.s_dq, 1)
                m.rel("s0set")
                ntoff += NW
                item += 1
    wsrc = t.get("wqkv_f32") or t.get("wo_f32")
    wdst = t.get("WQB") if t.get("WQB") is not None else t.get("WOB")
    C = wsrc.shape[1]
    for blk in range(wsrc.shape[0] // 128):
        sl = m.acq("gpsimd", "s0cast", 2, t.s_cw)
        m.dma("gpsimd", t.wcast[sl][0:128, 0:C],
              wsrc[128 * blk:128 * (blk + 1), :], t.s_cw)
        m.wait("gpsimd", t.s_cw, m.c[t.s_cw])
        m.dma("gpsimd", wdst[128 * blk:128 * (blk + 1), :],
              t.wcast[sl][0:128, 0:C], t.s_cw)
        m.rel("s0cast")
    m.wait("gpsimd", t.s_wr, m.c[t.s_wr])
    m.wait("gpsimd", t.s_cw, m.c[t.s_cw])
    m.barrier(t.bar)


def ln_schedule(t, m, x_dram, tt, xln_dst):
    m.wait("tensor", t.f_ln, m.c[t.f_ln])
    pending = []
    xslots = {}
    dma_at = {}
    for c in range(NKC + 1):
        if c < NKC:
            xsl = m.acq("gpsimd", "lnx", 2, t.f_cast)
            m.dma("gpsimd", t.xch[xsl][:, :],
                  bass.AP(x_dram, (128 * c) * TOK + T * tt, [[TOK, 128], [1, T]]),
                  t.f_x)
            xslots[c] = xsl
        else:
            m.dma("gpsimd", t.lnxdummy[:, :],
                  bass.AP(x_dram, 0, [[TOK, 128], [1, T]]), t.f_x)
        dma_at[c] = m.c[t.f_x]
        if c >= 1:
            cc0 = c - 1
            xsl = xslots.pop(cc0)
            m.wait("scalar", t.f_x, dma_at[cc0 + 1])
            qsl = m.acq("scalar", "lnq", 3, t.f_smm)
            m.op("scalar", lambda e, cc0=cc0, xsl=xsl: e.activation(
                xln_dst[:, T * cc0:T * (cc0 + 1)], t.xch[xsl][:, :], AF.Copy))
            m.op("scalar", lambda e, xsl=xsl, qsl=qsl: e.activation(
                t.xsq[qsl][:, :], t.xch[xsl][:, :], AF.Square), t.f_cast, 1)
            m.rel("lnx")
            pending.append((cc0, qsl, m.c[t.f_cast]))
            if cc0 == NKC - 1:
                m.pad("scalar", 2)
                m.op("scalar", lambda e: e.activation(
                    t.padS[1][0:1, 0:1], t.padS[0][0:1, 0:1], AF.Copy), t.f_cast, 1)
            while pending and (len(pending) > 1 or cc0 == NKC - 1):
                cc, qq, cast_at = pending.pop(0)
                tgt = pending[0][2] if pending else m.c[t.f_cast]
                m.wait("tensor", t.f_cast, tgt)
                m.op("tensor", lambda e, cc=cc: e.matmul(
                    t.pMB[0:1, :], t.ones_bf[:, 0:1], xln_dst[:, T * cc:T * (cc + 1)],
                    start=(cc == 0), stop=(cc == NKC - 1)))
                m.op("tensor", lambda e, cc=cc, qq=qq: e.matmul(
                    t.pRB[0:1, :], t.ones_bf[:, 0:1], t.xsq[qq][:, :],
                    start=(cc == 0), stop=(cc == NKC - 1)), t.f_smm, 1)
                m.rel("lnq")
    m.wait("vector", t.f_smm, m.c[t.f_smm])
    m.op("vector", lambda e: e.tensor_scalar(
        t.mrow[0:1, :], t.pMB[0:1, :], 1.0 / HID, None, ALU.mult))
    m.op("vector", lambda e: e.tensor_scalar(
        t.vrow[0:1, :], t.pRB[0:1, :], 1.0 / HID, None, ALU.mult))
    m.op("vector", lambda e: e.tensor_tensor(
        t.m2row[0:1, :], t.mrow[0:1, :], t.mrow[0:1, :], ALU.mult))
    m.op("vector", lambda e: e.tensor_tensor(
        t.vrow2[0:1, :], t.vrow[0:1, :], t.m2row[0:1, :], ALU.subtract))
    m.pad("vector", 2)
    m.op("vector", lambda e: e.tensor_copy(
        t.mark[0:1, 0:1], t.mark[0:1, 1:2]), t.f_row, 1)
    m.wait("scalar", t.f_row, m.c[t.f_row])
    m.op("scalar", lambda e: e.activation(
        t.srow[0:1, :], t.vrow2[0:1, :], AF.Sqrt, bias=t.eps[0:1, 0:1]))
    m.pad("scalar", 1)
    m.op("scalar", lambda e: e.activation(
        t.padS[1][0:1, 0:1], t.padS[0][0:1, 0:1], AF.Copy), t.f_sq, 1)
    m.wait("vector", t.f_sq, m.c[t.f_sq])
    m.op("vector", lambda e: e.reciprocal(t.rrow[0:1, :], t.srow[0:1, :]))
    m.op("vector", lambda e: e.tensor_copy(t.mhi[0:1, :], t.mrow[0:1, :]))
    m.op("vector", lambda e: e.tensor_tensor(
        t.mlo[0:1, :], t.mrow[0:1, :], t.mhi[0:1, :], ALU.subtract))
    m.op("vector", lambda e: e.tensor_copy(t.rhi[0:1, :], t.rrow[0:1, :]))
    m.op("vector", lambda e: e.tensor_tensor(
        t.rlo[0:1, :], t.rrow[0:1, :], t.rhi[0:1, :], ALU.subtract))
    m.pad("vector", 2)
    m.op("vector", lambda e: e.tensor_copy(
        t.mark[0:1, 0:1], t.mark[0:1, 1:2]), t.f_row, 1)
    m.wait("tensor", t.f_row, m.c[t.f_row])
    m.op("tensor", lambda e: e.matmul(
        t.pMB[:, :], t.ones1b[0:1, :], t.mhi[0:1, :], start=True, stop=False))
    m.op("tensor", lambda e: e.matmul(
        t.pMB[:, :], t.ones1b[0:1, :], t.mlo[0:1, :], start=False, stop=True))
    m.op("tensor", lambda e: e.matmul(
        t.pRB[:, :], t.ones1b[0:1, :], t.rhi[0:1, :], start=True, stop=False))
    m.op("tensor", lambda e: e.matmul(
        t.pRB[:, :], t.ones1b[0:1, :], t.rlo[0:1, :], start=False, stop=True),
        t.f_bc, 1)
    m.wait("vector", t.f_bc, m.c[t.f_bc])
    m.wait("gpsimd", t.f_cast, m.c[t.f_cast])
    x2slots = {}
    dma2_at = {}
    for c in range(NKC + 1):
        if c < NKC:
            x2sl = m.acq("gpsimd", "lnx2", 2, t.f_ln)
            m.dma("gpsimd", t.xch[x2sl][:, :],
                  bass.AP(x_dram, (128 * c) * TOK + T * tt, [[TOK, 128], [1, T]]),
                  t.f_x2)
            x2slots[c] = x2sl
        else:
            m.dma("gpsimd", t.lnxdummy[:, :],
                  bass.AP(x_dram, 0, [[TOK, 128], [1, T]]), t.f_x2)
        dma2_at[c] = m.c[t.f_x2]
        if c >= 1:
            cc = c - 1
            lsl = cc % 2
            x2sl = x2slots.pop(cc)
            m.wait("vector", t.f_x2, dma2_at[cc + 1])
            m.op("vector", lambda e, cc=cc, lsl=lsl, x2sl=x2sl: e.tensor_tensor(
                t.lntmp[lsl][:, :], t.xch[x2sl][:, :], t.pMB[:, :],
                ALU.subtract))
            m.op("vector", lambda e, cc=cc, lsl=lsl: e.tensor_tensor(
                xln_dst[:, T * cc:T * (cc + 1)], t.lntmp[lsl][:, :], t.pRB[:, :],
                ALU.mult), t.f_ln, 1)
            m.rel("lnx2")
    m.pad("vector", 2)
    m.op("vector", lambda e: e.tensor_copy(
        t.mark[0:1, 0:1], t.mark[0:1, 1:2]), t.f_ln, 1)


def ffn_schedule(t, m, x_dram, xout_dram, tt):
    ln_schedule(t, m, x_dram, tt, t.xln)
    ln_done = m.c[t.f_ln]
    for ic in range(NIC):
        Pi = CH_I[ic][2]
        sl = m.acq("gpsimd", "gust", 2, t.f_gu)
        m.dma("sync", t.gst[sl][:, :],
              bass.AP(t.CG, ic * 128 * CGROW, [[CGROW, 128], [1, CGROW]]), t.f_gs)
        m.dma("sync", t.ust[sl][:, :],
              bass.AP(t.CU, ic * 128 * CGROW, [[CGROW, 128], [1, CGROW]]), t.f_gs)
        m.wait("tensor", t.f_gs, m.c[t.f_gs])
        if ic == 0:
            m.wait("tensor", t.f_ln, ln_done)
        psl = m.acq("tensor", "gupp", 2, t.f_h)
        for kc in range(NKC):
            m.op("tensor", lambda e, kc=kc, Pi=Pi, sl=sl, psl=psl: e.matmul(
                t.pg[psl][0:Pi, :], t.gst[sl][:, 128 * kc:128 * kc + Pi],
                t.xln[:, T * kc:T * (kc + 1)],
                start=(kc == 0), stop=(kc == NKC - 1)))
        for kc in range(NKC):
            m.op("tensor", lambda e, kc=kc, Pi=Pi, sl=sl, psl=psl: e.matmul(
                t.pu[psl][0:Pi, :], t.ust[sl][:, 128 * kc:128 * kc + Pi],
                t.xln[:, T * kc:T * (kc + 1)],
                start=(kc == 0), stop=(kc == NKC - 1)),
                t.f_gu if kc == NKC - 1 else None, 1)
        m.rel("gust")
        m.wait("scalar", t.f_gu, m.c[t.f_gu])
        hsl = m.acq("scalar", "hsil", 2, t.f_h)
        m.op("scalar", lambda e, Pi=Pi, psl=psl, hsl=hsl: e.activation(
            t.hsil[hsl][0:Pi, :], t.pg[psl][0:Pi, :], AF.Silu), t.f_sil, 1)
        m.wait("vector", t.f_sil, m.c[t.f_sil])
        m.op("vector", lambda e, ic=ic, Pi=Pi, psl=psl, hsl=hsl: e.tensor_tensor(
            t.h[0:Pi, T * ic:T * (ic + 1)], t.hsil[hsl][0:Pi, :],
            t.pu[psl][0:Pi, :], ALU.mult), t.f_h, 1)
        m.rel("gupp")
        m.rel("hsil")
    m.pad("vector", 2)
    m.op("vector", lambda e: e.tensor_copy(
        t.mark[0:1, 0:1], t.mark[0:1, 1:2]), t.f_hf, 1)
    h_done = m.c[t.f_hf]
    if DEBUG_FFN and t.get("dbgh_d") is not None:
        m.wait("gpsimd", t.f_hf, h_done)
        if tt == 0:
            m.dma("gpsimd", t.dbgh_d[:, :], t.h[:, :], t.s_wr)
        m.dma("gpsimd", t.dbgxln_d[tt][:, :], t.xln[:, :], t.s_wr)
    QKC = NIC // 4
    QW = QKC * ICW
    for oc in range(NKC):
        xsl = m.acq("gpsimd", "xev", 2, t.f_yo)
        m.dma("scalar", t.xev[xsl][:, :],
              bass.AP(x_dram, (128 * oc) * TOK + T * tt, [[TOK, 128], [1, T]]),
              t.f_xe)
        if oc == 0:
            m.wait("tensor", t.f_hf, h_done)
        psl = m.acq("tensor", "dpp", 2, t.f_yo)
        for qtr in range(4):
            sl = m.acq("gpsimd", "dst", 2, t.f_y)
            m.dma("scalar", t.dstg[sl][:, :],
                  bass.AP(t.CD, oc * 128 * CDROW + QW * qtr,
                          [[CDROW, 128], [1, QW]]), t.f_ds)
            m.wait("tensor", t.f_ds, m.c[t.f_ds])
            for kk in range(QKC):
                kc = QKC * qtr + kk
                Pk = CH_I[kc][2]
                m.op("tensor", lambda e, kc=kc, kk=kk, Pk=Pk, sl=sl, psl=psl:
                     e.matmul(t.py[psl][:, :],
                              t.dstg[sl][0:Pk, 128 * kk:128 * (kk + 1)],
                              t.h[0:Pk, T * kc:T * (kc + 1)],
                              start=(kc == 0), stop=(kc == NIC - 1)),
                     t.f_y if kk == QKC - 1 else None, 1)
            m.rel("dst")
        m.wait("vector", t.f_y, m.c[t.f_y])
        m.wait("vector", t.f_xe, m.c[t.f_xe])
        osl = m.acq("vector", "xo", 2, t.f_wr)
        m.op("vector", lambda e, psl=psl, xsl=xsl, osl=osl: e.tensor_tensor(
            t.xo[osl][:, :], t.py[psl][:, :], t.xev[xsl][:, :], ALU.add),
            t.f_yo, 1)
        m.rel("xev")
        m.rel("dpp")
        m.wait("gpsimd", t.f_yo, m.c[t.f_yo])
        m.dma("gpsimd", bass.AP(xout_dram, (128 * oc) * TOK + T * tt,
                                [[TOK, 128], [1, T]]),
              t.xo[osl][:, :], t.f_wr)
        m.rel("xo")


def qkv_schedule(t, m, x_dram, tt):
    ln_schedule(t, m, x_dram, tt, t.xln)
    ln_done = m.c[t.f_ln]
    for ocg in range(6):
        for kc in range(NKC):
            sl = m.acq("gpsimd", "wqst", 2, t.q_mm)
            m.dma("sync", t.wqst[sl][:, :],
                  bass.AP(t.WQB, (128 * kc) * QKVD + 512 * ocg,
                          [[QKVD, 128], [1, 512]]), t.q_ws)
            m.wait("tensor", t.q_ws, m.c[t.q_ws])
            if ocg == 0 and kc == 0:
                m.wait("tensor", t.f_ln, ln_done)
            if kc == 0:
                m.wait("tensor", t.q_ev, m.c[t.q_ev])
            for o6 in range(4):
                m.op("tensor", lambda e, o6=o6, kc=kc, sl=sl: e.matmul(
                    t.pqkv[:, 512 * o6:512 * (o6 + 1)],
                    t.wqst[sl][:, 128 * o6:128 * (o6 + 1)],
                    t.xln[:, T * kc:T * (kc + 1)],
                    start=(kc == 0), stop=(kc == NKC - 1)),
                    t.q_mm if (kc == NKC - 1 and o6 == 3) else None, 1)
            m.rel("wqst")
        m.wait("scalar", t.q_mm, m.c[t.q_mm])
        for o6 in range(4):
            oc = 4 * ocg + o6
            dst = (t.qfm, t.kfm, t.vfm)[oc // 8]
            col = (oc % 8) * TOK + T * tt
            m.op("scalar", lambda e, o6=o6, oc=oc, dst=dst, col=col: e.activation(
                dst[:, col:col + T], t.pqkv[:, 512 * o6:512 * (o6 + 1)],
                AF.Identity, bias=t.bqkv_sb[:, oc:oc + 1]),
                t.q_ev if o6 == 3 else None, 1)


def attn_schedule(t, m):
    G = 2
    for qg in range(TOK // (128 * G)):
        exp_base = m.c[t.a_sm]
        tr_base = m.c[t.a_pt]
        for kt in range(S // T):
            sl = m.acq("gpsimd", "ktile", 2, t.a_s)
            m.dma("sync", bass.AP(t.ktile[sl], 0, [[8 * T, 128], [T, 8], [1, T]]),
                  bass.AP(t.K_d, T * kt, [[S, 128], [128 * S, 8], [1, T]]), t.a_k)
            m.wait("tensor", t.a_k, m.c[t.a_k])
            for qi in range(G):
                if kt == 0 and qi == 0:
                    m.wait("vector", t.a_sm, exp_base)
                psl = m.acq("tensor", "psc", 2, t.a_s)
                for dc in range(8):
                    q0 = TOK * dc + 128 * (G * qg + qi)
                    m.op("tensor", lambda e, dc=dc, q0=q0, sl=sl, psl=psl: e.matmul(
                        t.psc[psl][:, :], t.qfm[:, q0:q0 + 128],
                        t.ktile[sl][:, T * dc:T * (dc + 1)],
                        start=(dc == 0), stop=(dc == 7)),
                        t.a_mm if dc == 7 else None, 1)
                m.wait("vector", t.a_mm, m.c[t.a_mm])
                m.op("vector", lambda e, qi=qi, kt=kt, psl=psl: e.tensor_copy(
                    t.scores[qi][:, T * kt:T * (kt + 1)], t.psc[psl][:, :]),
                    t.a_s, 1)
                m.rel("psc")
            m.rel("ktile")
        for qi in range(G):
            m.op("vector", lambda e, qi=qi: e.reduce_max(
                t.negm[qi][:, :], t.scores[qi][:, :], axis=AX, negate=True))
            m.pad("vector", 1)
            m.op("vector", lambda e: e.tensor_copy(
                t.mark[0:1, 0:1], t.mark[0:1, 1:2]), t.a_sx, 1)
            m.wait("scalar", t.a_sx, m.c[t.a_sx])
            m.wait("scalar", t.a_pt, tr_base)
            m.op("scalar", lambda e, qi=qi: e.activation(
                t.probs[qi][:, :], t.scores[qi][:, :], AF.Exp,
                bias=t.negm[qi][:, :], accum_out=t.sums[qi][:, :]), t.a_sm, 1)
            m.wait("vector", t.a_sm, m.c[t.a_sm])
            m.pad("vector", 1)
            m.op("vector", lambda e, qi=qi: e.reciprocal(
                t.rec[qi][:, :], t.sums[qi][:, :]), t.a_rc, 1)
        sm_done = m.c[t.a_sm]
        for kc in range(S // 128):
            vsl = m.acq("gpsimd", "vtile", 2, t.a_avm)
            m.dma("scalar", t.vtile[vsl][:, :],
                  bass.AP(t.V_d, 128 * kc * D, [[D, 128], [1, D]]), t.a_v)
            for qi in range(G):
                if kc == 0 and qi == 0:
                    m.wait("tensor", t.a_sm, sm_done)
                tsl = m.acq("tensor", "ptr", 2, t.a_po)
                m.op("tensor", lambda e, qi=qi, kc=kc, tsl=tsl: e.matmul(
                    t.ptr[tsl][:, :], t.probs[qi][:, 128 * kc:128 * (kc + 1)],
                    t.idn[:, :], start=True, stop=True), t.a_pt, 1)
                m.wait("vector", t.a_pt, m.c[t.a_pt])
                ssl = m.acq("vector", "ptrs", 2, t.a_avm)
                m.op("vector", lambda e, tsl=tsl, ssl=ssl: e.tensor_copy(
                    t.ptrs[ssl][:, :], t.ptr[tsl][:, :]))
                m.pad("vector", 1)
                m.op("vector", lambda e: e.tensor_copy(
                    t.mark[0:1, 0:1], t.mark[0:1, 1:2]), t.a_po, 1)
                m.rel("ptr")
                m.wait("tensor", t.a_po, m.c[t.a_po])
                m.wait("tensor", t.a_v, m.c[t.a_v])
                for hf in range(2):
                    m.op("tensor", lambda e, qi=qi, kc=kc, hf=hf, vsl=vsl, ssl=ssl:
                         e.matmul(t.po[qi][:, 512 * hf:512 * (hf + 1)],
                                  t.ptrs[ssl][:, :],
                                  t.vtile[vsl][:, 512 * hf:512 * (hf + 1)],
                                  start=(kc == 0), stop=(kc == S // 128 - 1)),
                         t.a_avm if hf == 1 else None, 1)
                m.rel("ptrs")
            m.rel("vtile")
        m.wait("vector", t.a_avm, m.c[t.a_avm])
        m.wait("vector", t.a_rc, m.c[t.a_rc])
        for qi in range(G):
            m.op("vector", lambda e, qi=qi: e.tensor_scalar(
                t.outq[qi][:, :], t.po[qi][:, :], t.rec[qi][:, :], None, ALU.mult),
                None if qi == G - 1 else t.a_oq, 1)
        m.pad("vector", 2)
        m.op("vector", lambda e: e.tensor_copy(
            t.mark[0:1, 0:1], t.mark[0:1, 1:2]), t.a_oq, 1)
        m.wait("tensor", t.a_oq, m.c[t.a_oq])
        for qi in range(G):
            for dc in range(8):
                tsl = m.acq("tensor", "ptr", 2, t.a_po)
                m.op("tensor", lambda e, qi=qi, dc=dc, tsl=tsl: e.matmul(
                    t.ptr[tsl][:, :], t.outq[qi][:, 128 * dc:128 * (dc + 1)],
                    t.idn[:, :], start=True, stop=True), t.a_pt, 1)
                m.wait("vector", t.a_pt, m.c[t.a_pt])
                col = TOK * dc + 128 * (G * qg + qi)
                m.op("vector", lambda e, col=col, tsl=tsl: e.tensor_copy(
                    t.outfm[:, col:col + 128], t.ptr[tsl][:, :]), t.a_po, 1)
                m.rel("ptr")
    m.barrier(t.bar)


def wo_schedule(t, m, tt):
    if tt == 0:
        for dc in range(8):
            m.dma("sync", t.wost[dc][:, :],
                  bass.AP(t.WOB, (128 * dc) * HID, [[HID, 128], [1, HID]]), t.w_ws)
    ws = m.c[t.w_ws]
    for oc in range(NKC):
        xsl = m.acq("gpsimd", "wxev", 2, t.w_ev)
        m.dma("scalar", t.xev[xsl][:, :],
              bass.AP(t.x1, (128 * oc) * TOK + T * tt, [[TOK, 128], [1, T]]),
              t.f_xe)
        m.wait("tensor", t.w_ws, ws)
        psl = m.acq("tensor", "wpy", 2, t.w_ev)
        for dc in range(8):
            m.op("tensor", lambda e, dc=dc, oc=oc, psl=psl, tt=tt: e.matmul(
                t.py[psl][:, :], t.wost[dc][:, 128 * oc:128 * (oc + 1)],
                t.outfm[:, TOK * dc + T * tt:TOK * dc + T * (tt + 1)],
                start=(dc == 0), stop=(dc == 7)),
                t.w_mm if dc == 7 else None, 1)
        m.wait("scalar", t.w_mm, m.c[t.w_mm])
        wsl = m.acq("scalar", "wtmp", 2, t.w_ev)
        m.op("scalar", lambda e, oc=oc, psl=psl, wsl=wsl: e.activation(
            t.wtmp[wsl][:, :], t.py[psl][:, :], AF.Identity,
            bias=t.bo_sb[:, oc:oc + 1]), t.w_bias, 1)
        m.wait("vector", t.w_bias, m.c[t.w_bias])
        m.wait("vector", t.f_xe, m.c[t.f_xe])
        osl = m.acq("vector", "wxo", 2, t.f_wr)
        m.op("vector", lambda e, wsl=wsl, xsl=xsl, osl=osl: e.tensor_tensor(
            t.xo[osl][:, :], t.wtmp[wsl][:, :], t.xev[xsl][:, :], ALU.add),
            t.w_ev, 1)
        m.rel("wxev")
        m.rel("wpy")
        m.rel("wtmp")
        m.wait("gpsimd", t.w_ev, m.c[t.w_ev])
        m.dma("gpsimd", bass.AP(t.X2, (128 * oc) * TOK + T * tt,
                                [[TOK, 128], [1, T]]),
              t.xo[osl][:, :], t.f_wr)
        m.rel("wxo")


SEM_NAMES = ("bar", "c_ld", "s_ld", "s_cr", "s_bc", "s_dq", "s_wt", "s_wr",
             "s_cw", "f_x", "f_cast", "f_smm", "f_row", "f_sq", "f_bc", "f_ln",
             "f_gs", "f_gu", "f_sil", "f_h", "f_hf", "f_ds", "f_y", "f_xe", "f_x2", "f_yo",
             "f_wr", "q_ws", "q_mm", "q_ev", "q_fin", "q_wr", "v_tr", "v_cp",
             "a_k", "a_mm", "a_s", "a_sx", "a_sm", "a_rc", "a_v", "a_pt",
             "a_po", "a_ev", "a_oq", "a_avm", "w_ws", "w_mm", "w_bias", "w_ev", "s_dbg")


def _alloc_common(nc, st, t, counts):
    for nm in SEM_NAMES:
        s = Sem(st.enter_context(nc.semaphore(nm)), nm)
        counts[s] = 0
        setattr(t, nm, s)
    t.ind8f = _sb(nc, st, [8, 128], F32)
    t.ind8b = _sb(nc, st, [8, 128], BF16)
    t.ones_bf = _sb(nc, st, [128, 1], BF16)
    t.ones1f = _sb(nc, st, [1, 128], F32)
    t.ones1b = _sb(nc, st, [1, 128], BF16)
    t.eps = _sb(nc, st, [1, 1], F32)
    t.idn = _sb(nc, st, [128, 128], BF16)
    t.mark = _sb(nc, st, [1, 2], F32)
    t.padV = [_sb(nc, st, [128, 128], F32) for _ in range(2)]
    t.padS = [_sb(nc, st, [128, 128], F32) for _ in range(2)]
    PAD_TILES["V"] = t.padV
    PAD_TILES["S"] = t.padS

    def cload(m):
        m.dma("sync", t.ind8f[:, :], t.ind8f_d[:, :], t.c_ld)
        m.dma("sync", t.ind8b[:, :], t.ind8b_d[:, :], t.c_ld)
        m.dma("sync", t.ones_bf[:, :], t.ones_bf_d[:, :], t.c_ld)
        m.dma("sync", t.ones1f[:, :], t.ones1f_d[:, :], t.c_ld)
        m.dma("sync", t.ones1b[:, :], t.ones1b_d[:, :], t.c_ld)
        m.dma("sync", t.idn[:, :], t.idn_d[:, :], t.c_ld)
        m.op("vector", lambda e: e.memset(t.eps[:, :], 1e-5))
        m.op("vector", lambda e: e.memset(t.mark[:, :], 0.0))
        for en in ENGINES:
            m.wait(en, t.c_ld, None)
        m.barrier(t.bar)
    run_block(nc, counts, cload)


def _alloc_stage0(nc, st, t, wq):
    t.plane = [[_sb(nc, st, [128, 2048], U8) for _ in range(4)] for _ in range(2)]
    t.srows = [_sb(nc, st, [8, 2048], F32) for _ in range(2)]
    t.srows2 = [_sb(nc, st, [8, 2048], F32) for _ in range(2)]
    t.qzt = [_sb(nc, st, [8, 1376], I32) for _ in range(2)]
    t.zall = _sb(nc, st, [8, 2048], I32)
    t.crows = [_sb(nc, st, [8, 2048], BF16) for _ in range(2)]
    t.shi = [_sb(nc, st, [8, 2048], BF16) for _ in range(2)]
    t.slo = [_sb(nc, st, [8, 2048], BF16) for _ in range(2)]
    t.vu8 = _sb(nc, st, [128, 2048], U8)
    t.vadd = _sb(nc, st, [128, 2048], BF16)
    t.wt = [_sb(nc, st, [128, 2048], BF16) for _ in range(4)]
    t.p_sb = _ps(nc, st, [128, 2048], F32)
    t.p_cb = _ps(nc, st, [128, 2048], F32)
    t.wcast = [_sb(nc, st, [128, QKVD if wq else HID], BF16) for _ in range(2)]
    if DEBUG_S0:
        t.dbg_cr = _sb(nc, st, [8, 2048], F32)
        t.dbg_cb = _sb(nc, st, [128, 2048], F32)
        t.dbg_sb2 = _sb(nc, st, [128, 2048], F32)
        t.dbg_za = _sb(nc, st, [8, 2048], I32)
        t.dbg_qz = _sb(nc, st, [8, 1376], I32)


def _alloc_ln(nc, st, t):
    t.xln = _sb(nc, st, [128, NKC * T], BF16)
    t.xch = [_sb(nc, st, [128, T], F32) for _ in range(2)]
    t.xsq = [_sb(nc, st, [128, T], BF16) for _ in range(3)]
    t.lntmp = [_sb(nc, st, [128, T], F32) for _ in range(2)]
    t.lnxdummy = _sb(nc, st, [128, T], F32)
    t.mrow = _sb(nc, st, [1, T], F32)
    t.vrow = _sb(nc, st, [1, T], F32)
    t.vrow2 = _sb(nc, st, [1, T], F32)
    t.m2row = _sb(nc, st, [1, T], F32)
    t.srow = _sb(nc, st, [1, T], F32)
    t.rrow = _sb(nc, st, [1, T], F32)
    t.mhi = _sb(nc, st, [1, T], BF16)
    t.mlo = _sb(nc, st, [1, T], BF16)
    t.rhi = _sb(nc, st, [1, T], BF16)
    t.rlo = _sb(nc, st, [1, T], BF16)
    t.pMB = _ps(nc, st, [128, T], F32)
    t.pRB = _ps(nc, st, [128, T], F32)


def _alloc_ffn(nc, st, t):
    _alloc_ln(nc, st, t)
    t.gst = [_sb(nc, st, [128, CGROW], BF16) for _ in range(2)]
    t.ust = [_sb(nc, st, [128, CGROW], BF16) for _ in range(2)]
    t.pg = [_ps(nc, st, [128, T], F32) for _ in range(2)]
    t.pu = [_ps(nc, st, [128, T], F32) for _ in range(2)]
    t.py = [_ps(nc, st, [128, T], F32) for _ in range(2)]
    t.hsil = [_sb(nc, st, [128, T], F32) for _ in range(2)]
    t.h = _sb(nc, st, [128, NIC * T], BF16)
    t.dstg = [_sb(nc, st, [128, (NIC // 4) * ICW], BF16) for _ in range(2)]
    t.xev = [_sb(nc, st, [128, T], F32) for _ in range(2)]
    t.xo = [_sb(nc, st, [128, T], F32) for _ in range(2)]


def emit_launch_a(nc, io):
    counts = {}
    with ExitStack() as st:
        t = _Ctx()
        t.update(io)
        _alloc_common(nc, st, t, counts)
        with ExitStack() as st0:
            _alloc_stage0(nc, st0, t, wq=True)
            run_block(nc, counts, lambda m: stage0_schedule(t, m))
        with ExitStack() as st1:
            _alloc_ffn(nc, st1, t)

            def sched(m):
                for tt in range(NTT):
                    ffn_schedule(t, m, t.xin, t.x1, tt)
                if DEBUG_STOP == "ffn1":
                    m.wait("gpsimd", t.f_wr, m.c[t.f_wr])
                m.barrier(t.bar)
            run_block(nc, counts, sched)
        if DEBUG_STOP == "ffn1":
            return
        with ExitStack() as st2:
            _alloc_ln(nc, st2, t)
            t.wqst = [_sb(nc, st2, [128, 512], BF16) for _ in range(2)]
            t.pqkv = _ps(nc, st2, [128, 2048], F32)
            t.qfm = _sb(nc, st2, [128, 8 * TOK], BF16)
            t.kfm = _sb(nc, st2, [128, 8 * TOK], BF16)
            t.vfm = _sb(nc, st2, [128, 8 * TOK], BF16)
            t.bqkv_sb = _sb(nc, st2, [128, 24], F32)
            t.ptr = [_ps(nc, st2, [128, 128], F32) for _ in range(2)]
            t.vtr = _sb(nc, st2, [128, D], BF16)

            A_PHASE = os.environ.get("MK_A_PHASE", "full")

            def sched2(m):
                for oc in range(24):
                    m.dma("sync", t.bqkv_sb[:, oc:oc + 1],
                          bass.AP(t.bqkv_d, 128 * oc, [[1, 128], [1, 1]]), t.c_ld)
                for en in ENGINES:
                    m.wait(en, t.c_ld, None)
                m.barrier(t.bar)
                if A_PHASE == "ln":
                    for tt in range(NTT):
                        ln_schedule(t, m, t.x1, tt, t.xln)
                    m.wait("gpsimd", t.f_wr, m.c[t.f_wr])
                    return
                for tt in range(NTT):
                    qkv_schedule(t, m, t.x1, tt)
                m.wait("vector", t.q_ev, m.c[t.q_ev])
                m.op("vector", lambda e: e.tensor_scalar(
                    t.qfm[:, :], t.qfm[:, :], 1.0 / 32.0, None, ALU.mult),
                    t.q_fin, 1)
                m.wait("tensor", t.q_ev, m.c[t.q_ev])
                m.pad("scalar", 2)
                m.op("scalar", lambda e: e.activation(
                    t.padS[1][0:1, 0:1], t.padS[0][0:1, 0:1], AF.Copy), t.q_ev, 1)
                m.wait("tensor", t.q_ev, m.c[t.q_ev])
                m.wait("gpsimd", t.q_fin, m.c[t.q_fin])
                for dc in range(8):
                    m.dma("gpsimd",
                          bass.AP(t.q_out, (128 * dc) * TOK, [[TOK, 128], [1, TOK]]),
                          t.qfm[:, TOK * dc:TOK * (dc + 1)], t.q_wr)
                    m.dma("gpsimd",
                          bass.AP(t.k_out, (128 * dc) * TOK, [[TOK, 128], [1, TOK]]),
                          t.kfm[:, TOK * dc:TOK * (dc + 1)], t.q_wr)
                if A_PHASE == "qkv":
                    m.wait("gpsimd", t.q_wr, m.c[t.q_wr])
                    m.wait("gpsimd", t.f_wr, m.c[t.f_wr])
                    return
                for qt in range(TOK // 128):
                    for dc in range(8):
                        tsl = m.acq("tensor", "vptr", 2, t.v_cp)
                        c0 = TOK * dc + 128 * qt
                        m.op("tensor", lambda e, c0=c0, tsl=tsl: e.matmul(
                            t.ptr[tsl][:, :], t.vfm[:, c0:c0 + 128], t.idn[:, :],
                            start=True, stop=True),
                            t.v_tr, 1)
                        m.wait("vector", t.v_tr, m.c[t.v_tr])
                        m.op("vector", lambda e, dc=dc, tsl=tsl: e.tensor_copy(
                            t.vtr[:, 128 * dc:128 * (dc + 1)], t.ptr[tsl][:, :]),
                            t.v_cp, 1)
                        m.rel("vptr")
                    m.wait("gpsimd", t.v_cp, m.c[t.v_cp])
                    m.dma("gpsimd",
                          bass.AP(t.v_out, (128 * qt) * D, [[D, 128], [1, D]]),
                          t.vtr[:, :], t.q_wr)
                    m.wait("vector", t.q_wr, m.c[t.q_wr])
                m.wait("gpsimd", t.q_wr, m.c[t.q_wr])
                m.wait("gpsimd", t.f_wr, m.c[t.f_wr])
            run_block(nc, counts, sched2)


def emit_launch_b(nc, io):
    counts = {}
    with ExitStack() as st:
        t = _Ctx()
        t.update(io)
        _alloc_common(nc, st, t, counts)
        with ExitStack() as st0:
            _alloc_stage0(nc, st0, t, wq=False)
            run_block(nc, counts, lambda m: stage0_schedule(t, m))
        stow = st.enter_context(ExitStack())
        t.outfm = _sb(nc, stow, [128, 8 * TOK], BF16)
        with ExitStack() as st1:
            t.qfm = _sb(nc, st1, [128, 8 * TOK], BF16)
            t.ktile = [_sb(nc, st1, [128, 8 * T], BF16) for _ in range(2)]
            t.vtile = [_sb(nc, st1, [128, D], BF16) for _ in range(2)]
            t.scores = [_sb(nc, st1, [128, S], F32) for _ in range(2)]
            t.probs = [_sb(nc, st1, [128, S], BF16) for _ in range(2)]
            t.negm = [_sb(nc, st1, [128, 1], F32) for _ in range(2)]
            t.sums = [_sb(nc, st1, [128, 1], F32) for _ in range(2)]
            t.rec = [_sb(nc, st1, [128, 1], F32) for _ in range(2)]
            t.outq = [_sb(nc, st1, [128, D], BF16) for _ in range(2)]
            t.ptrs = [_sb(nc, st1, [128, 128], BF16) for _ in range(2)]
            t.psc = [_ps(nc, st1, [128, T], F32) for _ in range(2)]
            t.po = [_ps(nc, st1, [128, D], F32) for _ in range(2)]
            t.ptr = [_ps(nc, st1, [128, 128], F32) for _ in range(2)]

            def qload(m):
                for dc in range(8):
                    m.dma("sync", t.qfm[:, TOK * dc:TOK * (dc + 1)],
                          bass.AP(t.q_d, (128 * dc) * TOK, [[TOK, 128], [1, TOK]]),
                          t.c_ld)
                for en in ENGINES:
                    m.wait(en, t.c_ld, None)
                m.barrier(t.bar)
                attn_schedule(t, m)
            run_block(nc, counts, qload)
        with ExitStack() as st2:
            t.wost = [_sb(nc, st2, [128, HID], BF16) for _ in range(8)]
            t.wtmp = [_sb(nc, st2, [128, T], F32) for _ in range(2)]
            t.bo_sb = _sb(nc, st2, [128, 32], F32)
            t.xev = [_sb(nc, st2, [128, T], F32) for _ in range(2)]
            t.xo = [_sb(nc, st2, [128, T], F32) for _ in range(2)]
            t.py = [_ps(nc, st2, [128, T], F32) for _ in range(2)]

            def woall(m):
                for oc in range(32):
                    m.dma("sync", t.bo_sb[:, oc:oc + 1],
                          bass.AP(t.bo_d, 128 * oc, [[1, 128], [1, 1]]), t.c_ld)
                for en in ENGINES:
                    m.wait(en, t.c_ld, None)
                m.barrier(t.bar)
                for tt in range(NTT):
                    wo_schedule(t, m, tt)
                m.barrier(t.bar)
            run_block(nc, counts, woall)
        stow.close()
        with ExitStack() as st3:
            _alloc_ffn(nc, st3, t)

            def ffn2(m):
                for tt in range(NTT):
                    ffn_schedule(t, m, t.X2, t.xout, tt)
                m.wait("gpsimd", t.f_wr, m.c[t.f_wr])
            run_block(nc, counts, ffn2)


def _declare_quant(nc, io):
    for mat, K, N in (("gate", HID, INTER), ("up", HID, INTER), ("down", INTER, HID)):
        io[f"{mat}_bp"] = nc.dram_tensor(f"{mat}_bp", [K // 2, N], U8, kind="ExternalInput")
        io[f"{mat}_sc"] = nc.dram_tensor(f"{mat}_sc", [K // 128, N], F32, kind="ExternalInput")
        io[f"{mat}_qz"] = nc.dram_tensor(f"{mat}_qz", [K // 128, N // 8], I32, kind="ExternalInput")


def _declare_consts(nc, io):
    io["ind8f_d"] = nc.dram_tensor("ind8f", [8, 128], F32, kind="ExternalInput")
    io["ind8b_d"] = nc.dram_tensor("ind8b", [8, 128], BF16, kind="ExternalInput")
    io["ones_bf_d"] = nc.dram_tensor("ones_bf", [128, 1], BF16, kind="ExternalInput")
    io["ones1f_d"] = nc.dram_tensor("ones1f", [1, 128], F32, kind="ExternalInput")
    io["ones1b_d"] = nc.dram_tensor("ones1b", [1, 128], BF16, kind="ExternalInput")
    io["idn_d"] = nc.dram_tensor("idn", [128, 128], BF16, kind="ExternalInput")


def build_a():
    nc = bass.Bass("TRN2", target_bir_lowering=False)
    io = _Ctx()
    io["xin"] = nc.dram_tensor("xin", [HID, TOK], F32, kind="ExternalInput")
    if DEBUG_FFN:
        io["dbgh_d"] = nc.dram_tensor("dbgh", [128, NIC * T], BF16, kind="ExternalOutput")
        io["dbgxln_d"] = [nc.dram_tensor(f"dbgxln{i}", [128, NKC * T], BF16,
                                         kind="ExternalOutput") for i in range(NTT)]
    else:
        io["dbgh_d"] = None
        io["dbgxln_d"] = None
    if DEBUG_S0:
        io["dbgcr_d"] = nc.dram_tensor("dbgcr", [8, 2048], F32, kind="ExternalOutput")
        io["dbgcb_d"] = nc.dram_tensor("dbgcb", [128, 2048], F32, kind="ExternalOutput")
        io["dbgsb_d"] = nc.dram_tensor("dbgsb", [128, 2048], F32, kind="ExternalOutput")
        io["dbgza_d"] = nc.dram_tensor("dbgza", [8, 2048], I32, kind="ExternalOutput")
        io["dbgqz_d"] = nc.dram_tensor("dbgqz", [8, 1376], I32, kind="ExternalOutput")
    _declare_quant(nc, io)
    io["wqkv_f32"] = nc.dram_tensor("wqkv", [HID, QKVD], F32, kind="ExternalInput")
    io["bqkv_d"] = nc.dram_tensor("bqkv", [QKVD], F32, kind="ExternalInput")
    _declare_consts(nc, io)
    io["x1"] = nc.dram_tensor("x1", [HID, TOK], F32, kind="ExternalOutput")
    io["q_out"] = nc.dram_tensor("q_out", [D, TOK], BF16, kind="ExternalOutput")
    io["k_out"] = nc.dram_tensor("k_out", [D, TOK], BF16, kind="ExternalOutput")
    io["v_out"] = nc.dram_tensor("v_out", [TOK, D], BF16, kind="ExternalOutput")
    ckind = "ExternalOutput" if DEBUG_CACHE else "Internal"
    io["CG"] = nc.dram_tensor("CG", [NIC * 128 * CGROW], BF16, kind=ckind)
    io["CU"] = nc.dram_tensor("CU", [NIC * 128 * CGROW], BF16, kind=ckind)
    io["CD"] = nc.dram_tensor("CD", [NKC * 128 * CDROW], BF16, kind=ckind)
    io["WQB"] = nc.dram_tensor("WQB", [HID, QKVD], BF16)
    io["WOB"] = None
    io["wo_f32"] = None
    emit_launch_a(nc, io)
    return nc


def build_b():
    nc = bass.Bass("TRN2", target_bir_lowering=False)
    io = _Ctx()
    io["x1"] = nc.dram_tensor("x1", [HID, TOK], F32, kind="ExternalInput")
    io["q_d"] = nc.dram_tensor("q_in", [D, TOK], BF16, kind="ExternalInput")
    io["K_d"] = nc.dram_tensor("K_in", [D, S], BF16, kind="ExternalInput")
    io["V_d"] = nc.dram_tensor("V_in", [S, D], BF16, kind="ExternalInput")
    _declare_quant(nc, io)
    io["wo_f32"] = nc.dram_tensor("wo", [D, HID], F32, kind="ExternalInput")
    io["bo_d"] = nc.dram_tensor("bo", [HID], F32, kind="ExternalInput")
    _declare_consts(nc, io)
    io["xout"] = nc.dram_tensor("xout", [HID, TOK], F32, kind="ExternalOutput")
    io["CG"] = nc.dram_tensor("CG", [NIC * 128 * CGROW], BF16)
    io["CU"] = nc.dram_tensor("CU", [NIC * 128 * CGROW], BF16)
    io["CD"] = nc.dram_tensor("CD", [NKC * 128 * CDROW], BF16)
    io["WOB"] = nc.dram_tensor("WOB", [D, HID], BF16)
    io["WQB"] = None
    io["wqkv_f32"] = None
    io["X2"] = nc.dram_tensor("X2", [HID, TOK], F32)
    emit_launch_b(nc, io)
    return nc


def _consts():
    ind8f = np.zeros((8, 128), np.float32)
    for u in range(8):
        ind8f[u, 16 * u:16 * (u + 1)] = 1.0
    return dict(
        ind8f=ind8f,
        ind8b=ind8f.astype(ml_dtypes.bfloat16),
        ones_bf=np.ones((128, 1), ml_dtypes.bfloat16),
        ones1f=np.ones((1, 128), np.float32),
        ones1b=np.ones((1, 128), ml_dtypes.bfloat16),
        idn=np.eye(128, dtype=ml_dtypes.bfloat16),
    )


def kernel(L0_attn_x, L0_attn_Wqkv, L0_attn_bqkv, L0_attn_Wo, L0_attn_bo,
           L0_ffn_gate_qweight, L0_ffn_gate_scales, L0_ffn_gate_qzeros, L0_ffn_gate_g_idx,
           L0_ffn_up_qweight, L0_ffn_up_scales, L0_ffn_up_qzeros, L0_ffn_up_g_idx,
           L0_ffn_down_qweight, L0_ffn_down_scales, L0_ffn_down_qzeros, L0_ffn_down_g_idx):
    x = np.asarray(L0_attn_x, np.float32)
    assert (np.asarray(L0_ffn_gate_g_idx) == np.arange(HID) // 128).all()
    assert (np.asarray(L0_ffn_up_g_idx) == np.arange(HID) // 128).all()
    assert (np.asarray(L0_ffn_down_g_idx) == np.arange(INTER) // 128).all()

    quant = {}
    for mat, qw, sc, qz, perm_n in (
            ("gate", L0_ffn_gate_qweight, L0_ffn_gate_scales, L0_ffn_gate_qzeros, PERM_I),
            ("up", L0_ffn_up_qweight, L0_ffn_up_scales, L0_ffn_up_qzeros, PERM_I),
            ("down", L0_ffn_down_qweight, L0_ffn_down_scales, L0_ffn_down_qzeros, PERM_D)):
        quant[f"{mat}_bp"] = np.ascontiguousarray(
            byte_plane(np.ascontiguousarray(np.asarray(qw, np.int32)))[:, perm_n])
        quant[f"{mat}_sc"] = np.ascontiguousarray(np.asarray(sc, np.float32)[:, perm_n])
        quant[f"{mat}_qz"] = np.ascontiguousarray(np.asarray(qz, np.int32))
    consts = _consts()

    wqkv = np.ascontiguousarray(np.asarray(L0_attn_Wqkv, np.float32)[PERM_D, :])
    bqkv = np.ascontiguousarray(np.asarray(L0_attn_bqkv, np.float32))
    wo = np.ascontiguousarray(np.asarray(L0_attn_Wo, np.float32)[:, PERM_D])
    bo = np.ascontiguousarray(np.asarray(L0_attn_bo, np.float32)[PERM_D])

    in_maps_a = []
    for core in range(8):
        b, qs = divmod(core, 4)
        xs = np.ascontiguousarray(x[b, qs * TOK:(qs + 1) * TOK, :][:, PERM_D].T)
        in_maps_a.append(dict(xin=xs, wqkv=wqkv, bqkv=bqkv, **quant, **consts))

    nc_a = build_a()
    res_a = run_bass_kernel_spmd(nc_a, in_maps_a, core_ids=list(range(8))).results
    if DEBUG_STOP == "ffn1" or DEBUG_STOP == "a":
        return res_a

    in_maps_b = []
    for core in range(8):
        b = core // 4
        grp = range(4 * b, 4 * b + 4)
        K_full = np.ascontiguousarray(np.concatenate(
            [np.asarray(res_a[i]["k_out"]) for i in grp], axis=1))
        V_full = np.ascontiguousarray(np.concatenate(
            [np.asarray(res_a[i]["v_out"]) for i in grp], axis=0))
        in_maps_b.append(dict(
            x1=np.asarray(res_a[core]["x1"]), q_in=np.asarray(res_a[core]["q_out"]),
            K_in=K_full, V_in=V_full, wo=wo, bo=bo, **quant, **consts))

    nc_b = build_b()
    res_b = run_bass_kernel_spmd(nc_b, in_maps_b, core_ids=list(range(8))).results

    out = np.empty((2, S, HID), np.float32)
    for core in range(8):
        b, qs = divmod(core, 4)
        xo = np.asarray(res_b[core]["xout"])
        out[b, qs * TOK:(qs + 1) * TOK, :] = xo.T[:, INV_D]
    return out
